# version 06
# baseline (speedup 1.0000x reference)
import numpy as np
import ml_dtypes
import concourse.bass as bass
import concourse.mybir as mybir
from concourse.bass_utils import run_bass_kernel_spmd

F32 = mybir.dt.float32
BF16 = mybir.dt.bfloat16
AF = mybir.ActivationFunctionType
ALU = mybir.AluOpType
AX = mybir.AxisListType

TT = 512
NCH = 4
NSEQ = 2
DT = 16
ITE = 11
ITS = 22
NEXP = 8
NQ, NKV = 16, 4
NEG = -1e30
EPS_LN = 1e-5

N_INST = NSEQ * NCH
STOP_STAGE = None


def _pack_cols(W):
    W = np.ascontiguousarray(np.asarray(W), dtype=np.float32)
    K, M = W.shape
    nk, nm = K // 128, M // 128
    return np.ascontiguousarray(
        W.reshape(nk, 128, nm, 128).transpose(2, 1, 0, 3).reshape(nm, 128, nk * 128))


def _pack_vec(v, nt):
    v = np.ascontiguousarray(np.asarray(v), dtype=np.float32)
    return np.ascontiguousarray(v.reshape(nt, 128).T)


def _host_pack(inputs):
    p = {}
    p["wkT"] = _pack_cols(np.asarray(inputs["L0_attn_a_Wk"]).T)
    p["wvT"] = _pack_cols(np.asarray(inputs["L0_attn_a_Wv"]).T)
    p["wrT"] = _pack_cols(np.asarray(inputs["L0_attn_a_Wr"]).T)
    p["td"] = _pack_vec(inputs["L0_attn_a_time_decay"], DT)
    p["wq"] = _pack_cols(inputs["L0_attn_b_Wq"])
    p["wkb"] = _pack_cols(inputs["L0_attn_b_Wk"])
    p["wvb"] = _pack_cols(inputs["L0_attn_b_Wv"])
    p["wo"] = _pack_cols(inputs["L0_attn_b_Wo"])
    p["bq"] = _pack_vec(inputs["L0_attn_b_bq"], DT)
    p["bk"] = _pack_vec(inputs["L0_attn_b_bk"], NKV)
    p["bv"] = _pack_vec(inputs["L0_attn_b_bv"], NKV)
    p["bo"] = _pack_vec(inputs["L0_attn_b_bo"], DT)
    gw = np.asarray(inputs["L0_ffn_gate_w"], dtype=np.float32)
    p["gate"] = np.ascontiguousarray(
        gw.T.reshape(DT, 128, NEXP).transpose(1, 0, 2).reshape(128, DT * NEXP))
    p["w1"] = np.stack([_pack_cols(np.asarray(inputs["L0_ffn_w1"])[e]) for e in range(NEXP)])
    p["w3"] = np.stack([_pack_cols(np.asarray(inputs["L0_ffn_w3"])[e]) for e in range(NEXP)])
    p["w2"] = np.stack([_pack_cols(np.asarray(inputs["L0_ffn_w2"])[e]) for e in range(NEXP)])
    p["sw1"] = _pack_cols(inputs["L0_ffn_sw1"])
    p["sw3"] = _pack_cols(inputs["L0_ffn_sw3"])
    p["sw2"] = _pack_cols(inputs["L0_ffn_sw2"])
    return p


def _consts():
    c = {}
    pos = np.arange(NCH * TT, dtype=np.float64)
    freqs = 10000.0 ** (-2.0 * np.arange(64, dtype=np.float64) / 128.0)
    ang = pos[None, :] * np.repeat(freqs, 2)[:, None]
    sgn = np.where(np.arange(128) % 2 == 0, -1.0, 1.0)[:, None]
    s = 1.0 / np.sqrt(128.0)
    c["ropeqc"] = (np.cos(ang) * s).astype(ml_dtypes.bfloat16)
    c["ropeqs"] = (np.sin(ang) * sgn * s).astype(ml_dtypes.bfloat16)
    c["ropekc"] = np.cos(ang).astype(ml_dtypes.bfloat16)
    c["ropeks"] = (np.sin(ang) * sgn).astype(ml_dtypes.bfloat16)
    ii = np.arange(128)[:, None]
    jj512 = np.arange(512)[None, :]
    jj128 = np.arange(128)[None, :]
    c["mask_past"] = np.where(jj512 >= ii, 0.0, NEG).astype(ml_dtypes.bfloat16)
    c["mask_diag"] = np.where(jj128 <= ii, 0.0, NEG).astype(ml_dtypes.bfloat16)
    c["idf32"] = np.eye(128, dtype=np.float32)
    c["idb"] = np.eye(128, dtype=ml_dtypes.bfloat16)
    sw = np.zeros((128, 128), dtype=ml_dtypes.bfloat16)
    for j in range(128):
        sw[j ^ 1, j] = 1.0
    c["swap"] = sw
    oh = np.zeros((32, 32 * 128), dtype=ml_dtypes.bfloat16)
    for j in range(32):
        oh[j, j * 128:(j + 1) * 128] = 1.0
    c["onehot"] = oh
    c["ones1"] = np.ones((1, 128), dtype=ml_dtypes.bfloat16)
    c["onesdiv"] = np.full((128, 128), 1.0 / 2048.0, dtype=ml_dtypes.bfloat16)
    return c


class _Sem:
    __slots__ = ("name", "count", "h")

    def __init__(self, name):
        self.name, self.count, self.h = name, 0, None


class _Rec:
    KEYS = ("pe", "ve", "act", "sp", "gp")

    def __init__(self):
        self.ops = {k: [] for k in self.KEYS}
        self.lastwait = {k: {} for k in self.KEYS}

    def emit(self, eng, method, *args, wait=(), inc=None, **kwargs):
        w = []
        for item in wait:
            if item is None:
                continue
            sem, val = item
            if not val:
                continue
            assert val <= sem.count, (eng, method, sem.name, val, sem.count)
            if val > self.lastwait[eng].get(sem, 0):
                w.append((sem, val))
                self.lastwait[eng][sem] = val
        if method == "engine_nop" and inc is not None and not w:
            ops = self.ops[eng]
            if ops and ops[-1][0] is not None and ops[-1][4] is None:
                sem, amt = inc
                sem.count += amt
                m0, a0, k0, w0, _ = ops[-1]
                ops[-1] = (m0, a0, k0, w0, (sem, amt))
                return (sem, sem.count)
        rinc = None
        if inc is not None:
            sem, amt = inc
            sem.count += amt
            rinc = (sem, amt)
        self.ops[eng].append((method, args, kwargs, tuple(w), rinc))
        return None if rinc is None else (rinc[0], rinc[0].count)


class _Bank:
    __slots__ = ("ap", "release")

    def __init__(self, ap):
        self.ap, self.release = ap, None


class _Slot:
    __slots__ = ("t", "sem", "nloads", "release", "ready")

    def __init__(self, t, sem):
        self.t, self.sem = t, sem
        self.nloads, self.release, self.ready = 0, None, None


class _Ring:
    def __init__(self, slots, sems):
        self.slots = [_Slot(t, s) for t, s in zip(slots, sems)]
        self.i = 0

    def next(self):
        s = self.slots[self.i % len(self.slots)]
        self.i += 1
        return s


def _build_nc():
    nc = bass.Bass("TRN2", target_bir_lowering=False, debug=False,
                   detect_race_conditions=False)
    C = _consts()

    d_xT = nc.dram_tensor("xT", [NSEQ, 2048, NCH * TT], F32, kind="ExternalInput").ap()
    din = {}
    for name, shp in [
        ("wkT", [DT, 128, 2048]), ("wvT", [DT, 128, 2048]), ("wrT", [DT, 128, 2048]),
        ("td", [128, DT]),
        ("wq", [DT, 128, 2048]), ("wkb", [NKV, 128, 2048]), ("wvb", [NKV, 128, 2048]),
        ("wo", [DT, 128, 2048]),
        ("bq", [128, DT]), ("bk", [128, NKV]), ("bv", [128, NKV]), ("bo", [128, DT]),
        ("gate", [128, DT * NEXP]),
        ("w1", [NEXP, ITE, 128, 2048]), ("w3", [NEXP, ITE, 128, 2048]),
        ("w2", [NEXP, DT, 128, ITE * 128]),
        ("sw1", [ITS, 128, 2048]), ("sw3", [ITS, 128, 2048]), ("sw2", [DT, 128, ITS * 128]),
    ]:
        din[name] = nc.dram_tensor(name, shp, F32, kind="ExternalInput").ap()
    d_out = nc.dram_tensor("y", [NSEQ, 2048, NCH * TT], F32, kind="ExternalOutput").ap()
    wsc = {}
    for name in ("wkT", "wvT", "wrT", "wq", "wkb", "wvb", "wo", "w1", "w3",
                 "w2", "sw1", "sw3", "sw2"):
        shp = {"wkT": [DT, 128, 2048], "wvT": [DT, 128, 2048], "wrT": [DT, 128, 2048],
               "wq": [DT, 128, 2048], "wkb": [NKV, 128, 2048], "wvb": [NKV, 128, 2048],
               "wo": [DT, 128, 2048], "w1": [NEXP, ITE, 128, 2048],
               "w3": [NEXP, ITE, 128, 2048], "w2": [NEXP, DT, 128, ITE * 128],
               "sw1": [ITS, 128, 2048], "sw3": [ITS, 128, 2048],
               "sw2": [DT, 128, ITS * 128]}[name]
        wsc[name] = nc.dram_tensor(name + "_b", shp, BF16, kind="Internal").ap()

    dC = {k: nc.inline_tensor(v, "c_" + k).ap() for k, v in C.items()}

    sb = nc.alloc_sbuf_tensor
    x = [sb(f"x{i}", [128, TT], F32) for i in range(DT)]
    xnb = [sb(f"xnb{i}", [128, TT + 1], BF16) for i in range(DT)]
    scr = [sb(f"scr{i}", [128, TT], BF16) for i in range(DT)]
    kc = [sb(f"kc{g}", [128, NCH * TT], BF16) for g in range(NKV)]
    vc = [sb(f"vc{i}", [128, NKV * 128], BF16) for i in range(DT)]
    u_t = sb("u_t", [128, ITS * TT], BF16)
    ringK_t = [sb(f"rk{i}", [128, DT * 128], BF16) for i in range(6)]
    ring2_t = [sb(f"r2{i}", [128, ITS * 128], BF16) for i in range(2)]
    ek_sb = [sb(f"ek{i}", [128, TT], F32) for i in range(2)]
    a_sb = sb("a_sb", [128, TT], F32)
    num_sb = sb("num_sb", [128, TT], F32)
    den_sb = sb("den_sb", [128, TT], F32)
    silu_sb = [sb(f"silu{i}", [128, TT], F32) for i in range(2)]
    u0_sb = sb("u0_sb", [128, TT], F32)
    tmpA = sb("tmpA", [128, TT], F32)
    tmpB = sb("tmpB", [128, TT], F32)
    be_sb = sb("be_sb", [128, TT], F32)
    mean_b = sb("mean_b", [128, TT], F32)
    rstd_b = sb("rstd_b", [128, TT], F32)
    qkv_pre = sb("qkv_pre", [128, TT], BF16)
    q_h = [sb(f"qh{i}", [128, TT], BF16) for i in range(2)]
    p_sb = [sb(f"p{i}", [128, 640], BF16) for i in range(2)]
    pT_sb = [sb(f"pT{i}", [128, 640], BF16) for i in range(2)]
    r_small = sb("r_small", [128, 64], F32)
    r_acc = sb("r_acc", [128, 8], F32)
    m_sb = sb("m_sb", [128, 32], F32)
    mrt = sb("mrt", [128, 32], F32)
    mT_bf = sb("mT_bf", [32, 128], BF16)
    eqb = sb("eqb", [128, 8], F32)
    mskb = sb("mskb", [128, 8], F32)
    eshb = sb("eshb", [128, 8], F32)
    xtail = sb("xtail", [128, DT], BF16)
    carry_n = sb("carry_n", [128, DT], F32)
    carry_d = sb("carry_d", [128, DT], F32)
    w_sb = sb("w_sb", [128, DT], F32)
    td_sb = sb("td_sb", [128, DT], F32)
    bq_sb = sb("bq_sb", [128, DT], F32)
    bk_sb = sb("bk_sb", [128, NKV], F32)
    bv_sb = sb("bv_sb", [128, NKV], F32)
    bo_sb = sb("bo_sb", [128, DT], F32)
    gate_sb = sb("gate_sb", [128, DT * NEXP], BF16)
    ropeq_c = sb("ropeq_c", [128, TT], BF16)
    ropeq_s = sb("ropeq_s", [128, TT], BF16)
    ropek_c = sb("ropek_c", [128, TT], BF16)
    ropek_s = sb("ropek_s", [128, TT], BF16)
    mask_p = sb("mask_p", [128, 512], BF16)
    mask_d = sb("mask_d", [128, 128], BF16)
    idf_sb = sb("idf_sb", [128, 128], F32)
    idb_sb = sb("idb_sb", [128, 128], BF16)
    swap_sb = sb("swap_sb", [128, 128], BF16)
    oh_sb = sb("oh_sb", [32, 32 * 128], BF16)
    ones1_sb = sb("ones1_sb", [1, 128], BF16)
    onesdiv_sb = sb("onesdiv_sb", [128, 128], BF16)
    assert nc.sbuf_bytes_remaining >= 0

    psU0 = nc.alloc_psum_tensor("psU0", [128, 512], F32)
    psU1 = nc.alloc_psum_tensor("psU1", [128, 512], F32)
    psV0 = nc.alloc_psum_tensor("psV0", [128, 512], F32)
    psV1 = nc.alloc_psum_tensor("psV1", [128, 512], F32)
    psW0 = nc.alloc_psum_tensor("psW0", [128, 512], F32)
    psW1 = nc.alloc_psum_tensor("psW1", [128, 512], F32)
    psE = nc.alloc_psum_tensor("psE", [128, 1024], BF16)
    psF = nc.alloc_psum_tensor("psF", [128, 1024], BF16)
    U = [_Bank(psU0[:]), _Bank(psU1[:])]
    V = [_Bank(psV0[:]), _Bank(psV1[:])]
    W = [_Bank(psW0[:]), _Bank(psW1[:])]
    TB = [_Bank(psE[:]), _Bank(psF[:])]
    TB32 = [psE[:].bitcast(F32), psF[:].bitcast(F32)]

    def mksem(name):
        s = _Sem(name)
        s.h = nc.alloc_semaphore(name)
        return s

    s_pe, s_ve, s_act = mksem("s_pe"), mksem("s_ve"), mksem("s_act")
    s_pre, s_x, s_y, s_rope, s_gate = (mksem(n) for n in
                                       ("s_pre", "s_x", "s_y", "s_rope", "s_gate"))
    s_gpc = mksem("s_gpc")
    semK = [mksem(f"sK{i}") for i in range(6)]
    sem2 = [mksem(f"s2{i}") for i in range(2)]
    ringK = _Ring(ringK_t, semK)
    ring2 = _Ring(ring2_t, sem2)

    R = _Rec()

    def barrier(extra=()):
        snap = [(s_pe, s_pe.count), (s_ve, s_ve.count), (s_act, s_act.count)] + list(extra)
        for ek in ("pe", "ve", "act"):
            R.emit(ek, None, wait=snap)

    cast_done = {}

    def ensure_cast(name, e):
        key = (name, e)
        if key in cast_done:
            return cast_done[key]
        sem = mksem(f"sc_{name}{'' if e is None else e}")
        if e is None:
            R.emit("gp", "dma_start", wsc[name][:], din[name][:], inc=(sem, 16))
        else:
            R.emit("gp", "dma_start", wsc[name][e], din[name][e], inc=(sem, 16))
        cast_done[key] = (sem, 16)
        return cast_done[key]

    def load_col(ring, name, e, idx, width):
        cs = ensure_cast(name, e)
        slot = ring.next()
        src = wsc[name][idx] if e is None else wsc[name][e][idx]
        R.emit("gp", "dma_start", slot.t[:, 0:width], src,
               wait=[slot.release, cs], inc=(slot.sem, 16))
        slot.nloads += 1
        slot.ready = (slot.sem, 16 * slot.nloads)
        return slot

    pre = [(mask_p, dC["mask_past"]), (mask_d, dC["mask_diag"]), (idf_sb, dC["idf32"]),
           (idb_sb, dC["idb"]), (swap_sb, dC["swap"]), (oh_sb, dC["onehot"]),
           (ones1_sb, dC["ones1"]), (onesdiv_sb, dC["onesdiv"]),
           (td_sb, din["td"]), (bq_sb, din["bq"]), (bk_sb, din["bk"]),
           (bv_sb, din["bv"]), (bo_sb, din["bo"])]
    for dst, src in pre:
        R.emit("sp", "dma_start", dst[:], src[:], inc=(s_pre, 16))
    pre_io = (s_pre, s_pre.count)
    R.emit("gp", "dma_start", gate_sb[:], din["gate"][:], inc=(s_gate, 16))
    gate_val = (s_gate, 16)
    R.emit("ve", "memset", p_sb[0][:], 0.0)
    R.emit("ve", "memset", p_sb[1][:], 0.0, inc=(s_ve, 1))
    R.emit("act", "activation", w_sb[:], td_sb[:], AF.Exp, wait=[pre_io], inc=(s_act, 1))
    vw = R.emit("act", "activation", w_sb[:], w_sb[:], AF.Exp, scale=-1.0, inc=(s_act, 1))

    io_release = None
    rope_release = None
    ST = {"xnb_pe": None,
          "scr": None,
          "ln_act": None,
          "rstd": None,
          "xfin": [None] * DT,
          "a_rel": {},
          "u_rel": {}}

    def emit_ln(finalize=True, xload=None, sub=None):
        w0 = [xload] if xload else []
        GPC = set(range(10, DT))
        GPN = set(range(12, DT))
        cast_v, sq_v = [], []
        first = {"ve": True, "gp": True}
        for dt in range(DT):
            eng = "gp" if dt in GPC else "ve"
            sem = s_gpc if eng == "gp" else s_ve
            w = [ST["xfin"][dt]]
            if first[eng]:
                w += w0 + [ST["xnb_pe"], ST.get("tail")]
                first[eng] = False
            R.emit(eng, "tensor_copy", xnb[dt][:, 1:TT + 1], x[dt][:],
                   wait=w, inc=(sem, 1))
            cast_v.append((sem, sem.count))
        v_cast = cast_v[-1]
        for dt in range(DT):
            R.emit("act", "activation", scr[dt][:], x[dt][:], AF.Square,
                   wait=[ST["xfin"][dt]] + (w0 + [ST["scr"]] if dt == 0 else []),
                   inc=(s_act, 1))
            sq_v.append((s_act, s_act.count))
        a_sq = (s_act, s_act.count)
        if sub == 1:
            return R.emit("ve", "engine_nop", wait=[v_cast, a_sq], inc=(s_ve, 1))
        for dt in range(DT):
            R.emit("pe", "matmul", U[0].ap, onesdiv_sb[:], xnb[dt][:, 1:TT + 1],
                   start=(dt == 0), stop=(dt == DT - 1),
                   wait=[cast_v[dt]] + ([U[0].release, pre_io] if dt == 0 else []))
        g1 = R.emit("pe", "engine_nop", inc=(s_pe, 1))
        for dt in range(DT):
            R.emit("pe", "matmul", U[1].ap, onesdiv_sb[:], scr[dt][:],
                   start=(dt == 0), stop=(dt == DT - 1),
                   wait=[sq_v[dt]] + ([U[1].release] if dt == 0 else []))
        g2 = R.emit("pe", "engine_nop", inc=(s_pe, 1))
        R.emit("ve", "tensor_copy", mean_b[:], U[0].ap, wait=[g1])
        v_mean = R.emit("ve", "engine_nop", inc=(s_ve, 1))
        U[0].release = v_mean
        R.emit("ve", "tensor_mul", tmpA[:], mean_b[:], mean_b[:],
               wait=[ST["ln_act"]])
        R.emit("ve", "tensor_sub", tmpA[:], U[1].ap, tmpA[:], wait=[g2])
        v_var = R.emit("ve", "tensor_scalar_add", tmpA[:], tmpA[:], EPS_LN,
                       inc=(s_ve, 1))
        U[1].release = v_var
        if sub == 2:
            return v_var
        R.emit("act", "activation", rstd_b[:], tmpA[:], AF.Ln,
               wait=[v_var, ST["rstd"]])
        a_rstd = R.emit("act", "activation", rstd_b[:], rstd_b[:], AF.Exp,
                        scale=-0.5, inc=(s_act, 1))
        ST["ln_act"] = a_rstd
        ST["scr"] = g2
        first = {"ve": True, "gp": True}
        for dt in range(DT):
            eng = "gp" if dt in GPN else "ve"
            sem = s_gpc if eng == "gp" else s_ve
            tA = tmpB if eng == "ve" else a_sb
            w = [a_rstd, v_mean] if first[eng] else []
            first[eng] = False
            R.emit(eng, "tensor_sub", tA[:], x[dt][:], mean_b[:], wait=w)
            R.emit(eng, "tensor_mul", xnb[dt][:, 1:TT + 1], tA[:], rstd_b[:],
                   inc=(sem, 1))
        v_gp = (s_gpc, s_gpc.count)
        if finalize:
            return R.emit("ve", "engine_nop", inc=(s_ve, 1))
        return None

    def emit_stage_a(seq, ch, xload):
        if ch == 0:
            R.emit("ve", "memset", carry_n[:], 0.0, wait=[ST["xnb_pe"]])
            R.emit("ve", "memset", carry_d[:], 0.0)
            for dt in range(DT):
                R.emit("ve", "memset", xnb[dt][:, 0:1], 0.0)
        else:
            for dt in range(DT):
                R.emit("ve", "tensor_copy", xnb[dt][:, 0:1], xtail[:, dt:dt + 1],
                       wait=[ST["xnb_pe"]] if dt == 0 else ())
        v_gp = emit_ln(finalize=False, xload=xload)
        for dt in range(DT):
            R.emit("ve", "tensor_copy", xtail[:, dt:dt + 1], xnb[dt][:, TT:TT + 1],
                   wait=[v_gp] if dt == 0 else ())
        v_xn = R.emit("ve", "engine_nop", inc=(s_ve, 1))
        ST["rstd"] = v_xn

        rel = ST["a_rel"]
        for step in range(2 + DT):
            ce = step - 2
            if 0 <= ce < DT:
                gk, gv, gr, par = rel[ce]
                a_ek = rel[("ek", ce)]
                R.emit("ve", "tensor_mul", a_sb[:], V[par].ap, ek_sb[par][:],
                       wait=[gv, a_ek, vw])
                v_a = R.emit("ve", "engine_nop", inc=(s_ve, 1))
                V[par].release = v_a
                R.emit("ve", "tensor_tensor_scan", num_sb[:],
                       w_sb[:, ce:ce + 1].broadcast_to([128, TT]), a_sb[:],
                       carry_n[:, ce:ce + 1], ALU.mult, ALU.add)
                R.emit("ve", "tensor_tensor_scan", den_sb[:],
                       w_sb[:, ce:ce + 1].broadcast_to([128, TT]), ek_sb[par][:],
                       carry_d[:, ce:ce + 1], ALU.mult, ALU.add)
                v_scan = R.emit("ve", "engine_nop", inc=(s_ve, 1))
                rel[("ekfree", par)] = v_scan
                R.emit("ve", "tensor_copy", carry_n[:, ce:ce + 1], num_sb[:, TT - 1:TT])
                R.emit("ve", "tensor_copy", carry_d[:, ce:ce + 1], den_sb[:, TT - 1:TT])
                R.emit("ve", "tensor_scalar_add", den_sb[:], den_sb[:], 1e-8)
                R.emit("ve", "reciprocal", den_sb[:], den_sb[:])
                R.emit("ve", "tensor_mul", num_sb[:], num_sb[:], den_sb[:])
                R.emit("ve", "tensor_mul", tmpB[:], num_sb[:], W[par].ap, wait=[gr])
                v_fin = R.emit("ve", "tensor_add", x[ce][:], x[ce][:], tmpB[:],
                               inc=(s_ve, 1))
                W[par].release = v_fin
                ST["xfin"][ce] = v_fin
            if step < DT:
                et = step
                sk = load_col(ringK, "wkT", None, et, 2048)
                sv = load_col(ringK, "wvT", None, et, 2048)
                sr = load_col(ringK, "wrT", None, et, 2048)
                par = et % 2
                for kt in range(DT):
                    R.emit("pe", "matmul", U[par].ap, sk.t[:, kt * 128:(kt + 1) * 128],
                           xnb[kt][:, 0:TT], start=(kt == 0), stop=(kt == DT - 1),
                           wait=([sk.ready, v_xn, U[par].release]
                                 if kt == 0 else ()))
                gk = R.emit("pe", "engine_nop", inc=(s_pe, 1))
                sk.release = gk
                for kt in range(DT):
                    R.emit("pe", "matmul", V[par].ap, sv.t[:, kt * 128:(kt + 1) * 128],
                           xnb[kt][:, 0:TT], start=(kt == 0), stop=(kt == DT - 1),
                           wait=([sv.ready, V[par].release] if kt == 0 else ()))
                gv = R.emit("pe", "engine_nop", inc=(s_pe, 1))
                sv.release = gv
                for kt in range(DT):
                    R.emit("pe", "matmul", W[par].ap, sr.t[:, kt * 128:(kt + 1) * 128],
                           xnb[kt][:, 1:TT + 1], start=(kt == 0), stop=(kt == DT - 1),
                           wait=([sr.ready, W[par].release] if kt == 0 else ()))
                gr = R.emit("pe", "engine_nop", inc=(s_pe, 1))
                sr.release = gr
                rel[et] = (gk, gv, gr, par)
                R.emit("act", "activation", ek_sb[par][:], U[par].ap, AF.Exp,
                       wait=[gk, rel.get(("ekfree", par))])
                a_ek = R.emit("act", "engine_nop", inc=(s_act, 1))
                U[par].release = a_ek
                rel[("ek", et)] = a_ek
                if et == DT - 1:
                    ST["xnb_pe"] = gr

    def emit_ffn():
        v_xn = emit_ln()
        ST["rstd"] = v_xn
        for s in range(4):
            for kt in range(DT):
                R.emit("pe", "matmul", psW0[:, s * 8:(s + 1) * 8],
                       xnb[kt][:, 1 + s * 128:1 + (s + 1) * 128],
                       gate_sb[:, kt * NEXP:(kt + 1) * NEXP],
                       start=(kt == 0), stop=(kt == DT - 1),
                       wait=([v_xn, W[0].release, gate_val] if (s == 0 and kt == 0) else ()))
        glog = R.emit("pe", "engine_nop", inc=(s_pe, 1))
        R.emit("ve", "tensor_copy", mrt[:], psW0[:, 0:32], wait=[glog])
        v_mrt = R.emit("ve", "engine_nop", inc=(s_ve, 1))
        W[0].release = v_mrt
        def vch(method, *args, wait=(), **kw):
            return R.emit("ve", method, *args,
                          wait=list(wait) + [(s_ve, s_ve.count)],
                          inc=(s_ve, 1), **kw)

        for s in range(4):
            sl = mrt[:, s * 8:(s + 1) * 8]
            c0 = r_small[:, s * 8:s * 8 + 1]
            c1 = r_small[:, s * 8 + 1:s * 8 + 2]
            c2 = r_small[:, s * 8 + 2:s * 8 + 3]
            c3 = r_small[:, s * 8 + 3:s * 8 + 4]
            vch("reduce_max", c0, sl, axis=AX.X, negate=True)
            vch("tensor_scalar_mul", c1, c0, -1.0)
            vch("tensor_scalar", eqb[:], sl, c1, None, ALU.is_ge)
            vch("scalar_tensor_tensor", mskb[:], eqb[:], NEG, sl,
                ALU.mult, ALU.add)
            v_msk = vch("reduce_max", c2, mskb[:], axis=AX.X)
            a_esh = R.emit("act", "activation", eshb[:], sl, AF.Exp, bias=c0,
                           wait=[v_msk], inc=(s_act, 1))
            vch("tensor_scalar", eqb[:], sl, c2, None, ALU.is_ge)
            vch("tensor_mul", mskb[:], eqb[:], eshb[:], wait=[a_esh])
            vch("reduce_sum", c3, mskb[:], axis=AX.X)
            vch("reciprocal", c3, c3)
            vch("tensor_scalar_mul", m_sb[:, s * 8:(s + 1) * 8], mskb[:], c3)
        v_m = R.emit("ve", "engine_nop", inc=(s_ve, 1))
        g_mT = R.emit("pe", "transpose", psW0[0:32, 128:256], m_sb[:], idf_sb[:],
                      wait=[v_m, W[0].release], inc=(s_pe, 1))
        v_mtb = R.emit("ve", "tensor_copy", mT_bf[:], psW0[0:32, 128:256],
                       wait=[g_mT], inc=(s_ve, 1))
        W[0].release = v_mtb

        u_rel = ST["u_rel"]

        def emit_y_group(st, ht):
            par = ht % 2
            ring = ring2 if st["shared"] else ringK
            s2 = load_col(ring, "sw2" if st["shared"] else "w2",
                          None if st["shared"] else st["e"], ht, st["nit"] * 128)
            for it in range(st["nit"]):
                R.emit("pe", "matmul", W[par].ap, s2.t[:, it * 128:(it + 1) * 128],
                       u_t[:, (st["uoff"] + it) * TT:(st["uoff"] + it + 1) * TT],
                       start=(it == 0), stop=(it == st["nit"] - 1),
                       wait=([s2.ready, W[par].release, st["uval"]] if it == 0 else ()))
            gy = R.emit("pe", "engine_nop", inc=(s_pe, 1))
            s2.release = gy
            v_y = R.emit("ve", "tensor_add", x[ht][:], x[ht][:], W[par].ap,
                         wait=[gy], inc=(s_ve, 1))
            W[par].release = v_y
            ST["xfin"][ht] = v_y
            st["last_gy"] = gy
            if not st["q"]:
                for it in range(st["nit"]):
                    u_rel[st["uoff"] + it] = gy

        def drain_y(st, n):
            while st is not None and st["q"] and n > 0:
                emit_y_group(st, st["q"].pop(0))
                n -= 1

        pend = None
        for e in range(NEXP + 1):
            shared = (e == NEXP)
            nit = ITS if shared else ITE
            uoff = 0 if shared else (e % 2) * ITE
            if not shared:
                for s in range(4):
                    j = s * 8 + e
                    R.emit("pe", "matmul", TB32[0][:, s * 128:(s + 1) * 128],
                           oh_sb[:, j * 128:(j + 1) * 128], mT_bf[:],
                           start=True, stop=True,
                           wait=([v_mtb, TB[0].release] if s == 0 else ()))
                g_be = R.emit("pe", "engine_nop", inc=(s_pe, 1))
                v_be = R.emit("ve", "tensor_copy", be_sb[:], TB32[0][:, 0:TT],
                              wait=[g_be], inc=(s_ve, 1))
                TB[0].release = v_be
            for it in range(nit):
                par = it % 2
                s1 = load_col(ringK, "sw1" if shared else "w1", None if shared else e, it, 2048)
                for kt in range(DT):
                    R.emit("pe", "matmul", U[par].ap, s1.t[:, kt * 128:(kt + 1) * 128],
                           xnb[kt][:, 1:TT + 1], start=(kt == 0), stop=(kt == DT - 1),
                           wait=([s1.ready, v_xn, U[par].release] if kt == 0 else ()))
                gh = R.emit("pe", "engine_nop", inc=(s_pe, 1))
                s1.release = gh
                a_si = R.emit("act", "activation", silu_sb[par][:], U[par].ap, AF.Silu,
                              wait=[gh, u_rel.get(("silufree", par))], inc=(s_act, 1))
                U[par].release = a_si
                s3 = load_col(ringK, "sw3" if shared else "w3", None if shared else e, it, 2048)
                for kt in range(DT):
                    R.emit("pe", "matmul", V[par].ap, s3.t[:, kt * 128:(kt + 1) * 128],
                           xnb[kt][:, 1:TT + 1], start=(kt == 0), stop=(kt == DT - 1),
                           wait=([s3.ready, V[par].release] if kt == 0 else ()))
                gg = R.emit("pe", "engine_nop", inc=(s_pe, 1))
                s3.release = gg
                if shared and it == nit - 1:
                    ST["xnb_pe"] = gg
                drain_y(pend, 2)
                ut = u_t[:, (uoff + it) * TT:(uoff + it + 1) * TT]
                if shared:
                    v_u = R.emit("ve", "tensor_mul", ut, V[par].ap, silu_sb[par][:],
                                 wait=[gg, a_si, u_rel.get(uoff + it)], inc=(s_ve, 1))
                else:
                    R.emit("ve", "tensor_mul", u0_sb[:], V[par].ap, silu_sb[par][:],
                           wait=[gg, a_si])
                    v_u = R.emit("ve", "tensor_mul", ut, u0_sb[:], be_sb[:],
                                 wait=[u_rel.get(uoff + it)], inc=(s_ve, 1))
                V[par].release = v_u
                u_rel[("silufree", par)] = v_u
                u_rel[("uval", uoff + it)] = v_u
            drain_y(pend, DT)
            pend = {"e": e, "shared": shared, "nit": nit, "uoff": uoff,
                    "q": list(range(DT)), "last_gy": None,
                    "uval": u_rel[("uval", uoff + nit - 1)]}
        drain_y(pend, DT)
        return

    def emit_stage_c(seq, ch, rope_val):
        nonlocal rope_release
        v_xn = emit_ln()
        ST["rstd"] = v_xn
        kv_done = None
        for g in range(NKV):
            par = g % 2
            sk = load_col(ringK, "wkb", None, g, 2048)
            for kt in range(DT):
                R.emit("pe", "matmul", U[par].ap, sk.t[:, kt * 128:(kt + 1) * 128],
                       xnb[kt][:, 1:TT + 1], start=(kt == 0), stop=(kt == DT - 1),
                       wait=([sk.ready, v_xn, U[par].release] if kt == 0 else ()))
            gk = R.emit("pe", "engine_nop", inc=(s_pe, 1))
            sk.release = gk
            v_kp = R.emit("ve", "tensor_scalar", qkv_pre[:], U[par].ap,
                          bk_sb[:, g:g + 1], None, ALU.add,
                          wait=[gk, pre_io], inc=(s_ve, 1))
            g_sw = R.emit("pe", "matmul", U[par].ap, swap_sb[:], qkv_pre[:],
                          start=True, stop=True, wait=[v_kp], inc=(s_pe, 1))
            R.emit("ve", "tensor_mul", tmpA[:], qkv_pre[:], ropek_c[:],
                   wait=[rope_val])
            R.emit("ve", "tensor_mul", tmpB[:], U[par].ap, ropek_s[:], wait=[g_sw])
            v_kr = R.emit("ve", "tensor_add", kc[g][:, ch * TT:(ch + 1) * TT],
                          tmpA[:], tmpB[:], inc=(s_ve, 1))
            U[par].release = v_kr
            sv = load_col(ringK, "wvb", None, g, 2048)
            for kt in range(DT):
                R.emit("pe", "matmul", U[par].ap, sv.t[:, kt * 128:(kt + 1) * 128],
                       xnb[kt][:, 1:TT + 1], start=(kt == 0), stop=(kt == DT - 1),
                       wait=([sv.ready, U[par].release] if kt == 0 else ()))
            gv = R.emit("pe", "engine_nop", inc=(s_pe, 1))
            sv.release = gv
            v_vp = R.emit("ve", "tensor_scalar", qkv_pre[:], U[par].ap,
                          bv_sb[:, g:g + 1], None, ALU.add, wait=[gv], inc=(s_ve, 1))
            U[par].release = v_vp
            for s in range(4):
                R.emit("pe", "transpose", TB[par].ap[:, s * 128:(s + 1) * 128],
                       qkv_pre[:, s * 128:(s + 1) * 128], idb_sb[:],
                       wait=([v_vp, TB[par].release] if s == 0 else ()))
            g_vt = R.emit("pe", "engine_nop", inc=(s_pe, 1))
            for s in range(4):
                R.emit("ve", "tensor_copy", vc[ch * 4 + s][:, g * 128:(g + 1) * 128],
                       TB[par].ap[:, s * 128:(s + 1) * 128], wait=[g_vt] if s == 0 else ())
            kv_done = R.emit("ve", "engine_nop", inc=(s_ve, 1))
            TB[par].release = kv_done
        SC = [W, V]
        SCP = [psW0, psV0]
        SCD = [psW1, psV1]

        def emit_scores(h, s, v_qr):
            i = h * 4 + s
            P2 = i % 2
            qs = ch * TT + s * 128
            Wp = min(qs, 512)
            g = h // 4
            scb = SC[P2]
            sctp, sctd = SCP[P2], SCD[P2]
            qpar = h % 2
            if Wp:
                R.emit("pe", "matmul", sctp[:, 0:Wp],
                       q_h[qpar][:, s * 128:(s + 1) * 128],
                       kc[g][:, qs - Wp:qs], start=True, stop=(Wp != 512),
                       wait=[v_qr, kv_done, scb[0].release, scb[1].release, pre_io])
                if Wp == 512:
                    R.emit("pe", "matmul", sctp[:, 0:512], idb_sb[:], mask_p[:],
                           start=False, stop=True)
            R.emit("pe", "matmul", sctd[:, 0:128],
                   q_h[qpar][:, s * 128:(s + 1) * 128],
                   kc[g][:, qs:qs + 128], start=True, stop=False,
                   wait=([v_qr, kv_done, scb[0].release, scb[1].release, pre_io]
                         if not Wp else ()))
            R.emit("pe", "matmul", sctd[:, 0:128], idb_sb[:], mask_d[:],
                   start=False, stop=True)
            return R.emit("pe", "engine_nop", inc=(s_pe, 1))

        def emit_chain(h, s, g_sc):
            i = h * 4 + s
            P2 = i % 2
            qs = ch * TT + s * 128
            Wp = min(qs, 512)
            nj = Wp // 128
            g = h // 4
            scb = SC[P2]
            sctp, sctd = SCP[P2], SCD[P2]
            ac = r_acc
            if Wp:
                R.emit("ve", "reduce_max", ac[:, 4:5], sctp[:, 0:Wp], axis=AX.X,
                       wait=[g_sc], inc=(s_ve, 1))
                R.emit("ve", "reduce_max", ac[:, 5:6], sctd[:, 0:128], axis=AX.X,
                       inc=(s_ve, 1))
                R.emit("ve", "tensor_max", ac[:, 4:5], ac[:, 4:5], ac[:, 5:6],
                       wait=[(s_ve, s_ve.count)], inc=(s_ve, 1))
                v_mx = R.emit("ve", "tensor_scalar_mul", ac[:, 6:7], ac[:, 4:5],
                              -1.0, wait=[(s_ve, s_ve.count)], inc=(s_ve, 1))
            else:
                v_mx = R.emit("ve", "reduce_max", ac[:, 6:7], sctd[:, 0:128],
                              axis=AX.X, negate=True, wait=[g_sc], inc=(s_ve, 1))
            if Wp:
                R.emit("act", "activation", p_sb[P2][:, 0:Wp], sctp[:, 0:Wp],
                       AF.Exp, bias=ac[:, 6:7], accum_out=ac[:, 0:1], wait=[v_mx])
            a_exp = R.emit("act", "activation", p_sb[P2][:, 512:640],
                           sctd[:, 0:128], AF.Exp, bias=ac[:, 6:7],
                           accum_out=ac[:, 1:2],
                           wait=[v_mx] if not Wp else (), inc=(s_act, 1))
            scb[0].release = scb[1].release = a_exp
            if Wp:
                R.emit("ve", "tensor_add", ac[:, 2:3], ac[:, 0:1], ac[:, 1:2],
                       wait=[a_exp], inc=(s_ve, 1))
            else:
                R.emit("ve", "tensor_copy", ac[:, 2:3], ac[:, 1:2],
                       wait=[a_exp], inc=(s_ve, 1))
            R.emit("ve", "reciprocal", ac[:, 2:3], ac[:, 2:3],
                   wait=[(s_ve, s_ve.count)], inc=(s_ve, 1))
            v_p = R.emit("ve", "tensor_scalar_mul", p_sb[P2][:, 0:640],
                         p_sb[P2][:, 0:640], ac[:, 2:3],
                         wait=[(s_ve, s_ve.count)], inc=(s_ve, 1))
            for j in range(nj):
                R.emit("pe", "transpose", TB[P2].ap[:, j * 128:(j + 1) * 128],
                       p_sb[P2][:, j * 128:(j + 1) * 128], idb_sb[:],
                       wait=([v_p, TB[P2].release] if j == 0 else ()))
            R.emit("pe", "transpose", TB[P2].ap[:, 512:640],
                   p_sb[P2][:, 512:640], idb_sb[:],
                   wait=([v_p, TB[P2].release] if nj == 0 else ()))
            g_tr = R.emit("pe", "engine_nop", inc=(s_pe, 1))
            v_pt = R.emit("ve", "tensor_copy", pT_sb[P2][:, 0:640],
                          TB[P2].ap[:, 0:640], wait=[g_tr], inc=(s_ve, 1))
            vt0 = (qs - Wp) // 128
            for j in range(nj + 1):
                vt = vt0 + j if j < nj else qs // 128
                rhs = pT_sb[P2][:, j * 128:(j + 1) * 128] if j < nj else \
                    pT_sb[P2][:, 512:640]
                R.emit("pe", "matmul", TB32[P2][:, 384:512],
                       vc[vt][:, g * 128:(g + 1) * 128], rhs,
                       start=(j == 0), stop=(j == nj),
                       wait=[v_pt] if j == 0 else ())
            g_av = R.emit("pe", "engine_nop", inc=(s_pe, 1))
            v_at = R.emit("ve", "tensor_copy",
                          scr[h][:, s * 128:(s + 1) * 128], TB32[P2][:, 384:512],
                          wait=[g_av], inc=(s_ve, 1))
            TB[P2].release = v_at

        def qproj_start(h):
            return {"h": h, "qpar": h % 2, "kt": 0,
                    "sq": load_col(ringK, "wq", None, h, 2048)}

        def qproj_part(stq, nk):
            sq, qpar = stq["sq"], stq["qpar"]
            hi = min(stq["kt"] + nk, DT)
            for kt in range(stq["kt"], hi):
                R.emit("pe", "matmul", U[qpar].ap, sq.t[:, kt * 128:(kt + 1) * 128],
                       xnb[kt][:, 1:TT + 1], start=(kt == 0), stop=(kt == DT - 1),
                       wait=([sq.ready, v_xn, U[qpar].release] if kt == 0 else ()))
            stq["kt"] = hi
            if hi == DT and "gq" not in stq:
                stq["gq"] = R.emit("pe", "engine_nop", inc=(s_pe, 1))
                sq.release = stq["gq"]
                if stq["h"] == NQ - 1:
                    ST["xnb_pe"] = stq["gq"]

        def qproj_finish(stq):
            qproj_part(stq, DT)
            h, qpar, gq = stq["h"], stq["qpar"], stq["gq"]
            v_qp = R.emit("ve", "tensor_scalar", qkv_pre[:], U[qpar].ap,
                          bq_sb[:, h:h + 1], None, ALU.add, wait=[gq], inc=(s_ve, 1))
            g_sw = R.emit("pe", "matmul", U[qpar].ap, swap_sb[:], qkv_pre[:],
                          start=True, stop=True, wait=[v_qp], inc=(s_pe, 1))
            R.emit("ve", "tensor_mul", tmpA[:], qkv_pre[:], ropeq_c[:], wait=[rope_val])
            R.emit("ve", "tensor_mul", tmpB[:], U[qpar].ap, ropeq_s[:], wait=[g_sw])
            v_qr = R.emit("ve", "tensor_add", q_h[qpar][:], tmpA[:], tmpB[:],
                          inc=(s_ve, 1))
            U[qpar].release = v_qr
            return v_qr

        stq = qproj_start(0)
        v_qr = qproj_finish(stq)
        for h in range(NQ):
            nxt = qproj_start(h + 1) if h + 1 < NQ else None
            g_sc = emit_scores(h, 0, v_qr)
            for s in range(4):
                if nxt is not None:
                    qproj_part(nxt, 4)
                g_next = emit_scores(h, s + 1, v_qr) if s < 3 else None
                emit_chain(h, s, g_sc)
                g_sc = g_next
            if nxt is not None:
                v_qr = qproj_finish(nxt)
        rope_release = v_qr
        v_attn = R.emit("ve", "engine_nop", inc=(s_ve, 1))
        for et in range(DT):
            par = et % 2
            so = load_col(ringK, "wo", None, et, 2048)
            for kt in range(DT):
                R.emit("pe", "matmul", U[par].ap, so.t[:, kt * 128:(kt + 1) * 128],
                       scr[kt][:], start=(kt == 0), stop=(kt == DT - 1),
                       wait=([so.ready, v_attn, U[par].release] if kt == 0 else ()))
            go = R.emit("pe", "engine_nop", inc=(s_pe, 1))
            so.release = go
            v_o = R.emit("ve", "scalar_tensor_tensor", x[et][:], U[par].ap,
                         bo_sb[:, et:et + 1], x[et][:], ALU.add, ALU.add,
                         wait=[go], inc=(s_ve, 1))
            U[par].release = v_o
            ST["xfin"][et] = v_o
            ST["scr"] = go

    def emit_xload(inst, per_tile_after=None):
        seq, ch = divmod(inst, NCH)
        for dt in range(DT):
            w = []
            if per_tile_after is not None:
                w.append(per_tile_after[dt])
            elif dt == 0 and io_release is not None:
                w.append(io_release)
            R.emit("sp", "dma_start", x[dt][:],
                   d_xT[seq, dt * 128:(dt + 1) * 128, ch * TT:(ch + 1) * TT],
                   wait=w, inc=(s_x, 16))
        return (s_x, s_x.count)

    xload = emit_xload(0)
    for inst in range(N_INST):
        seq, ch = divmod(inst, NCH)
        for dst, src in [(ropeq_c, dC["ropeqc"]), (ropeq_s, dC["ropeqs"]),
                         (ropek_c, dC["ropekc"]), (ropek_s, dC["ropeks"])]:
            R.emit("sp", "dma_start", dst[:], src[:, ch * TT:(ch + 1) * TT],
                   wait=[rope_release] if dst is ropeq_c else (), inc=(s_rope, 16))
        rope_val = (s_rope, s_rope.count)

        if STOP_STAGE == "P":
            pass
        elif STOP_STAGE in ("L", "L1", "L2", "L3"):
            barrier(extra=[xload])
            emit_ln(finalize=True, xload=xload,
                    sub={"L1": 1, "L2": 2, "L3": 3}.get(STOP_STAGE))
        else:
            emit_stage_a(seq, ch, xload)
        if STOP_STAGE not in ("A", "P", "L", "L1", "L2", "L3"):
            emit_ffn()
            if STOP_STAGE != "B":
                emit_stage_c(seq, ch, rope_val)
                if STOP_STAGE != "C":
                    emit_ffn()
        if STOP_STAGE is None:
            after = []
            for dt in range(DT):
                R.emit("sp", "dma_start",
                       d_out[seq, dt * 128:(dt + 1) * 128, ch * TT:(ch + 1) * TT],
                       x[dt][:], wait=[ST["xfin"][dt]], inc=(s_y, 16))
                after.append((s_y, s_y.count))
            io_release = (s_y, s_y.count)
            if inst + 1 < N_INST:
                xload = emit_xload(inst + 1, per_tile_after=after)
        else:
            v_last = R.emit("ve", "engine_nop", inc=(s_ve, 1))
            for dt in range(DT):
                R.emit("sp", "dma_start",
                       d_out[seq, dt * 128:(dt + 1) * 128, ch * TT:(ch + 1) * TT],
                       x[dt][:], wait=[v_last] if dt == 0 else (), inc=(s_y, 16))
            io_release = (s_y, s_y.count)
            if inst + 1 < N_INST:
                xload = emit_xload(inst + 1)

    R.emit("sp", None, wait=[(s_y, s_y.count), (s_pre, s_pre.count),
                             (s_x, s_x.count), (s_rope, s_rope.count)])
    gp_fin = [(s_gate, s_gate.count)]
    gp_fin += [(s.sem, 16 * s.nloads) for s in ringK.slots + ring2.slots]
    R.emit("gp", None, wait=gp_fin)

    blk_cm = nc.Block()
    blk = blk_cm.__enter__()

    def _mk(engkey):
        def f(eng):
            for method, args, kwargs, waits, rinc in R.ops[engkey]:
                for sem, val in waits:
                    eng.wait_ge(sem.h, val)
                if method is None:
                    continue
                if method == "engine_nop":
                    ins = eng.nop()
                else:
                    ins = getattr(eng, method)(*args, **kwargs)
                if rinc is not None:
                    ins.then_inc(rinc[0].h, rinc[1])
        return f

    blk.tensor(_mk("pe"))
    blk.vector(_mk("ve"))
    blk.scalar(_mk("act"))
    blk.sync(_mk("sp"))
    blk.gpsimd(_mk("gp"))
    blk_cm.__exit__(None, None, None)
    return nc


def kernel(**inputs):
    xT = np.ascontiguousarray(
        np.asarray(inputs["L0_attn_a_x"], dtype=np.float32).transpose(0, 2, 1))
    packs = _host_pack(inputs)
    nc = _build_nc()
    in_maps = []
    for c in range(8):
        m = dict(packs)
        m["xT"] = np.ascontiguousarray(xT[2 * c:2 * c + 2])
        in_maps.append(m)
    res = run_bass_kernel_spmd(nc, in_maps, core_ids=list(range(8)))
    out = np.empty((16, 2048, 2048), dtype=np.float32)
    for c in range(8):
        out[2 * c:2 * c + 2] = res.results[c]["y"].transpose(0, 2, 1)
    return out


# version 07 (final)
# speedup vs baseline: 1.1246x; 1.1246x over previous
import numpy as np
import ml_dtypes
import concourse.bass as bass
import concourse.mybir as mybir
from concourse.bass_utils import run_bass_kernel_spmd

F32 = mybir.dt.float32
BF16 = mybir.dt.bfloat16
AF = mybir.ActivationFunctionType
ALU = mybir.AluOpType
AX = mybir.AxisListType

TT = 512
NCH = 4
NSEQ = 2
DT = 16
ITE = 11
ITS = 22
NEXP = 8
NQ, NKV = 16, 4
NEG = -1e30
EPS_LN = 1e-5

N_INST = NSEQ * NCH
STOP_STAGE = None


def _pack_cols(W):
    W = np.ascontiguousarray(np.asarray(W), dtype=np.float32)
    K, M = W.shape
    nk, nm = K // 128, M // 128
    return np.ascontiguousarray(
        W.reshape(nk, 128, nm, 128).transpose(2, 1, 0, 3).reshape(nm, 128, nk * 128))


def _pack_vec(v, nt):
    v = np.ascontiguousarray(np.asarray(v), dtype=np.float32)
    return np.ascontiguousarray(v.reshape(nt, 128).T)


def _host_pack(inputs):
    p = {}
    p["wkT"] = _pack_cols(np.asarray(inputs["L0_attn_a_Wk"]).T)
    p["wvT"] = _pack_cols(np.asarray(inputs["L0_attn_a_Wv"]).T)
    p["wrT"] = _pack_cols(np.asarray(inputs["L0_attn_a_Wr"]).T)
    p["td"] = _pack_vec(inputs["L0_attn_a_time_decay"], DT)
    p["wq"] = _pack_cols(inputs["L0_attn_b_Wq"])
    p["wkb"] = _pack_cols(inputs["L0_attn_b_Wk"])
    p["wvb"] = _pack_cols(inputs["L0_attn_b_Wv"])
    p["wo"] = _pack_cols(inputs["L0_attn_b_Wo"])
    p["bq"] = _pack_vec(inputs["L0_attn_b_bq"], DT)
    p["bk"] = _pack_vec(inputs["L0_attn_b_bk"], NKV)
    p["bv"] = _pack_vec(inputs["L0_attn_b_bv"], NKV)
    p["bo"] = _pack_vec(inputs["L0_attn_b_bo"], DT)
    gw = np.asarray(inputs["L0_ffn_gate_w"], dtype=np.float32)
    p["gate"] = np.ascontiguousarray(
        gw.T.reshape(DT, 128, NEXP).transpose(1, 0, 2).reshape(128, DT * NEXP))
    p["w1"] = np.stack([_pack_cols(np.asarray(inputs["L0_ffn_w1"])[e]) for e in range(NEXP)])
    p["w3"] = np.stack([_pack_cols(np.asarray(inputs["L0_ffn_w3"])[e]) for e in range(NEXP)])
    p["w2"] = np.stack([_pack_cols(np.asarray(inputs["L0_ffn_w2"])[e]) for e in range(NEXP)])
    p["sw1"] = _pack_cols(inputs["L0_ffn_sw1"])
    p["sw3"] = _pack_cols(inputs["L0_ffn_sw3"])
    p["sw2"] = _pack_cols(inputs["L0_ffn_sw2"])
    return p


def _consts():
    c = {}
    pos = np.arange(NCH * TT, dtype=np.float64)
    freqs = 10000.0 ** (-2.0 * np.arange(64, dtype=np.float64) / 128.0)
    ang = pos[None, :] * np.repeat(freqs, 2)[:, None]
    sgn = np.where(np.arange(128) % 2 == 0, -1.0, 1.0)[:, None]
    s = 1.0 / np.sqrt(128.0)
    c["ropeqc"] = (np.cos(ang) * s).astype(ml_dtypes.bfloat16)
    c["ropeqs"] = (np.sin(ang) * sgn * s).astype(ml_dtypes.bfloat16)
    c["ropekc"] = np.cos(ang).astype(ml_dtypes.bfloat16)
    c["ropeks"] = (np.sin(ang) * sgn).astype(ml_dtypes.bfloat16)
    ii = np.arange(128)[:, None]
    jj512 = np.arange(512)[None, :]
    jj128 = np.arange(128)[None, :]
    c["mask_past"] = np.where(jj512 >= ii, 0.0, NEG).astype(ml_dtypes.bfloat16)
    c["mask_diag"] = np.where(jj128 <= ii, 0.0, NEG).astype(ml_dtypes.bfloat16)
    c["idf32"] = np.eye(128, dtype=np.float32)
    c["idb"] = np.eye(128, dtype=ml_dtypes.bfloat16)
    sw = np.zeros((128, 128), dtype=ml_dtypes.bfloat16)
    for j in range(128):
        sw[j ^ 1, j] = 1.0
    c["swap"] = sw
    oh = np.zeros((32, 32 * 128), dtype=ml_dtypes.bfloat16)
    for j in range(32):
        oh[j, j * 128:(j + 1) * 128] = 1.0
    c["onehot"] = oh
    c["ones1"] = np.ones((1, 128), dtype=ml_dtypes.bfloat16)
    c["onesdiv"] = np.full((128, 128), 1.0 / 2048.0, dtype=ml_dtypes.bfloat16)
    return c


class _Sem:
    __slots__ = ("name", "count", "h")

    def __init__(self, name):
        self.name, self.count, self.h = name, 0, None


class _Rec:
    KEYS = ("pe", "ve", "act", "sp", "gp")

    def __init__(self):
        self.ops = {k: [] for k in self.KEYS}
        self.lastwait = {k: {} for k in self.KEYS}

    def emit(self, eng, method, *args, wait=(), inc=None, **kwargs):
        w = []
        for item in wait:
            if item is None:
                continue
            sem, val = item
            if not val:
                continue
            assert val <= sem.count, (eng, method, sem.name, val, sem.count)
            if val > self.lastwait[eng].get(sem, 0):
                w.append((sem, val))
                self.lastwait[eng][sem] = val
        if method == "engine_nop" and inc is not None and not w:
            ops = self.ops[eng]
            if ops and ops[-1][0] is not None and ops[-1][4] is None:
                sem, amt = inc
                sem.count += amt
                m0, a0, k0, w0, _ = ops[-1]
                ops[-1] = (m0, a0, k0, w0, (sem, amt))
                return (sem, sem.count)
        rinc = None
        if inc is not None:
            sem, amt = inc
            sem.count += amt
            rinc = (sem, amt)
        self.ops[eng].append((method, args, kwargs, tuple(w), rinc))
        return None if rinc is None else (rinc[0], rinc[0].count)


class _Bank:
    __slots__ = ("ap", "release")

    def __init__(self, ap):
        self.ap, self.release = ap, None


class _Slot:
    __slots__ = ("t", "sem", "nloads", "release", "ready")

    def __init__(self, t, sem):
        self.t, self.sem = t, sem
        self.nloads, self.release, self.ready = 0, None, None


class _Ring:
    def __init__(self, slots, sems):
        self.slots = [_Slot(t, s) for t, s in zip(slots, sems)]
        self.i = 0

    def next(self):
        s = self.slots[self.i % len(self.slots)]
        self.i += 1
        return s


def _build_nc():
    nc = bass.Bass("TRN2", target_bir_lowering=False, debug=False,
                   detect_race_conditions=False)
    C = _consts()

    d_xT = nc.dram_tensor("xT", [NSEQ, 2048, NCH * TT], F32, kind="ExternalInput").ap()
    din = {}
    for name, shp in [
        ("wkT", [DT, 128, 2048]), ("wvT", [DT, 128, 2048]), ("wrT", [DT, 128, 2048]),
        ("td", [128, DT]),
        ("wq", [DT, 128, 2048]), ("wkb", [NKV, 128, 2048]), ("wvb", [NKV, 128, 2048]),
        ("wo", [DT, 128, 2048]),
        ("bq", [128, DT]), ("bk", [128, NKV]), ("bv", [128, NKV]), ("bo", [128, DT]),
        ("gate", [128, DT * NEXP]),
        ("w1", [NEXP, ITE, 128, 2048]), ("w3", [NEXP, ITE, 128, 2048]),
        ("w2", [NEXP, DT, 128, ITE * 128]),
        ("sw1", [ITS, 128, 2048]), ("sw3", [ITS, 128, 2048]), ("sw2", [DT, 128, ITS * 128]),
    ]:
        din[name] = nc.dram_tensor(name, shp, F32, kind="ExternalInput").ap()
    d_out = nc.dram_tensor("y", [NSEQ, 2048, NCH * TT], F32, kind="ExternalOutput").ap()
    wsc = {}
    for name in ("wkT", "wvT", "wrT", "wq", "wkb", "wvb", "wo", "w1", "w3",
                 "w2", "sw1", "sw3", "sw2"):
        shp = {"wkT": [DT, 128, 2048], "wvT": [DT, 128, 2048], "wrT": [DT, 128, 2048],
               "wq": [DT, 128, 2048], "wkb": [NKV, 128, 2048], "wvb": [NKV, 128, 2048],
               "wo": [DT, 128, 2048], "w1": [NEXP, ITE, 128, 2048],
               "w3": [NEXP, ITE, 128, 2048], "w2": [NEXP, DT, 128, ITE * 128],
               "sw1": [ITS, 128, 2048], "sw3": [ITS, 128, 2048],
               "sw2": [DT, 128, ITS * 128]}[name]
        wsc[name] = nc.dram_tensor(name + "_b", shp, BF16, kind="Internal").ap()

    dC = {k: nc.inline_tensor(v, "c_" + k).ap() for k, v in C.items()}

    sb = nc.alloc_sbuf_tensor
    x = [sb(f"x{i}", [128, TT], F32) for i in range(DT)]
    xnbs = [[sb(f"xnb{p}_{i}", [128, TT + 1], BF16) for i in range(DT)]
            for p in range(2)]
    scr = [sb(f"scr{i}", [128, TT], BF16) for i in range(DT)]
    kc = [sb(f"kc{g}", [128, NCH * TT], BF16) for g in range(NKV)]
    vc = [sb(f"vc{i}", [128, NKV * 128], BF16) for i in range(DT)]
    u_t = sb("u_t", [128, ITS * TT], BF16)
    ringK_t = [sb(f"rk{i}", [128, DT * 128], BF16) for i in range(4)]
    ring2_t = [sb(f"r2{i}", [128, ITS * 128], BF16) for i in range(2)]
    ek_sb = [sb(f"ek{i}", [128, TT], F32) for i in range(2)]
    a_sb = sb("a_sb", [128, TT], F32)
    silu_sb = [sb(f"silu{i}", [128, TT], F32) for i in range(2)]
    tmpA = sb("tmpA", [128, TT], F32)
    tmpB = sb("tmpB", [128, TT], F32)
    be_sb = sb("be_sb", [128, TT], F32)
    mean_b = sb("mean_b", [128, TT], F32)
    rstd_b = sb("rstd_b", [128, TT], F32)
    qkv_pre = sb("qkv_pre", [128, TT], BF16)
    q_h = [sb(f"qh{i}", [128, TT], BF16) for i in range(2)]
    p_sb = [sb(f"p{i}", [128, 640], BF16) for i in range(2)]
    pT_sb = [sb(f"pT{i}", [128, 640], BF16) for i in range(2)]
    r_small = sb("r_small", [128, 64], F32)
    r_acc = sb("r_acc", [128, 8], F32)
    m_sb = sb("m_sb", [128, 32], F32)
    mrt = sb("mrt", [128, 32], F32)
    mT_bf = sb("mT_bf", [32, 128], BF16)
    eqb = sb("eqb", [128, 8], F32)
    mskb = sb("mskb", [128, 8], F32)
    eshb = sb("eshb", [128, 8], F32)
    xtail = sb("xtail", [128, DT], BF16)
    carry_n = sb("carry_n", [128, DT], F32)
    carry_d = sb("carry_d", [128, DT], F32)
    w_sb = sb("w_sb", [128, DT], F32)
    td_sb = sb("td_sb", [128, DT], F32)
    bq_sb = sb("bq_sb", [128, DT], F32)
    bk_sb = sb("bk_sb", [128, NKV], F32)
    bv_sb = sb("bv_sb", [128, NKV], F32)
    bo_sb = sb("bo_sb", [128, DT], F32)
    gate_sb = sb("gate_sb", [128, DT * NEXP], BF16)
    ropeq_c = sb("ropeq_c", [128, TT], BF16)
    ropeq_s = sb("ropeq_s", [128, TT], BF16)
    ropek_c = sb("ropek_c", [128, TT], BF16)
    ropek_s = sb("ropek_s", [128, TT], BF16)
    mask_p = sb("mask_p", [128, 512], BF16)
    mask_d = sb("mask_d", [128, 128], BF16)
    idf_sb = sb("idf_sb", [128, 128], F32)
    idb_sb = sb("idb_sb", [128, 128], BF16)
    swap_sb = sb("swap_sb", [128, 128], BF16)
    oh_sb = sb("oh_sb", [32, 32 * 128], BF16)
    onesdiv_sb = sb("onesdiv_sb", [128, 128], BF16)
    assert nc.sbuf_bytes_remaining >= 0

    psU0 = nc.alloc_psum_tensor("psU0", [128, 512], F32)
    psU1 = nc.alloc_psum_tensor("psU1", [128, 512], F32)
    psV0 = nc.alloc_psum_tensor("psV0", [128, 512], F32)
    psV1 = nc.alloc_psum_tensor("psV1", [128, 512], F32)
    psW0 = nc.alloc_psum_tensor("psW0", [128, 512], F32)
    psW1 = nc.alloc_psum_tensor("psW1", [128, 512], F32)
    psE = nc.alloc_psum_tensor("psE", [128, 1024], BF16)
    psF = nc.alloc_psum_tensor("psF", [128, 1024], BF16)
    U = [_Bank(psU0[:]), _Bank(psU1[:])]
    V = [_Bank(psV0[:]), _Bank(psV1[:])]
    W = [_Bank(psW0[:]), _Bank(psW1[:])]
    TB = [_Bank(psE[:]), _Bank(psF[:])]
    TB32 = [psE[:].bitcast(F32), psF[:].bitcast(F32)]

    def mksem(name):
        s = _Sem(name)
        s.h = nc.alloc_semaphore(name)
        return s

    s_pe, s_ve, s_act = mksem("s_pe"), mksem("s_ve"), mksem("s_act")
    s_pre, s_x, s_y, s_rope, s_gate = (mksem(n) for n in
                                       ("s_pre", "s_x", "s_y", "s_rope", "s_gate"))
    s_gpc = mksem("s_gpc")
    semK = [mksem(f"sK{i}") for i in range(4)]
    sem2 = [mksem(f"s2{i}") for i in range(2)]
    ringK = _Ring(ringK_t, semK)
    ring2 = _Ring(ring2_t, sem2)

    R = _Rec()

    def barrier(extra=()):
        snap = [(s_pe, s_pe.count), (s_ve, s_ve.count), (s_act, s_act.count)] + list(extra)
        for ek in ("pe", "ve", "act"):
            R.emit(ek, None, wait=snap)

    cast_done = {}

    def ensure_cast(name, e):
        key = (name, e)
        if key in cast_done:
            return cast_done[key]
        sem = mksem(f"sc_{name}{'' if e is None else e}")
        if e is None:
            R.emit("gp", "dma_start", wsc[name][:], din[name][:], inc=(sem, 16))
        else:
            R.emit("gp", "dma_start", wsc[name][e], din[name][e], inc=(sem, 16))
        cast_done[key] = (sem, 16)
        return cast_done[key]

    def load_col(ring, name, e, idx, width):
        cs = ensure_cast(name, e)
        slot = ring.next()
        src = wsc[name][idx] if e is None else wsc[name][e][idx]
        R.emit("gp", "dma_start", slot.t[:, 0:width], src,
               wait=[slot.release, cs], inc=(slot.sem, 16))
        slot.nloads += 1
        slot.ready = (slot.sem, 16 * slot.nloads)
        return slot

    pre = [(mask_p, dC["mask_past"]), (mask_d, dC["mask_diag"]), (idf_sb, dC["idf32"]),
           (idb_sb, dC["idb"]), (swap_sb, dC["swap"]), (oh_sb, dC["onehot"]),
           (onesdiv_sb, dC["onesdiv"]),
           (td_sb, din["td"]), (bq_sb, din["bq"]), (bk_sb, din["bk"]),
           (bv_sb, din["bv"]), (bo_sb, din["bo"])]
    for dst, src in pre:
        R.emit("sp", "dma_start", dst[:], src[:], inc=(s_pre, 16))
    pre_io = (s_pre, s_pre.count)
    R.emit("gp", "dma_start", gate_sb[:], din["gate"][:], inc=(s_gate, 16))
    gate_val = (s_gate, 16)
    R.emit("ve", "memset", p_sb[0][:], 0.0)
    R.emit("ve", "memset", p_sb[1][:], 0.0, inc=(s_ve, 1))
    R.emit("act", "activation", w_sb[:], td_sb[:], AF.Exp, wait=[pre_io], inc=(s_act, 1))
    vw = R.emit("act", "activation", w_sb[:], w_sb[:], AF.Exp, scale=-1.0, inc=(s_act, 1))

    io_release = None
    rope_release = None
    ST = {"xnb_pe": None,
          "scr": None,
          "ln_act": None,
          "rstd": None,
          "xfin": [None] * DT,
          "a_rel": {},
          "u_rel": {},
          "xnb": None}
    ln_ctr = [0]

    def emit_ln(finalize=True, xload=None, sub=None):
        w0 = [xload] if xload else []
        cur = xnbs[ln_ctr[0] % 2]
        ln_ctr[0] += 1
        ST["xnb"] = cur
        GPC = set(range(10, DT))
        GPN = set(range(12, DT))
        cast_v, sq_v = [], []
        first = {"ve": True, "gp": True}
        for dt in range(DT):
            eng = "gp" if dt in GPC else "ve"
            sem = s_gpc if eng == "gp" else s_ve
            w = [ST["xfin"][dt]]
            if first[eng]:
                w += w0
                first[eng] = False
            R.emit(eng, "tensor_copy", cur[dt][:, 1:TT + 1], x[dt][:],
                   wait=w, inc=(sem, 1))
            cast_v.append((sem, sem.count))
        v_cast = cast_v[-1]
        for dt in range(DT):
            R.emit("act", "activation", scr[dt][:], x[dt][:], AF.Square,
                   wait=[ST["xfin"][dt]] + (w0 + [ST["scr"]] if dt == 0 else []),
                   inc=(s_act, 1))
            sq_v.append((s_act, s_act.count))
        a_sq = (s_act, s_act.count)
        if sub == 1:
            return R.emit("ve", "engine_nop", wait=[v_cast, a_sq], inc=(s_ve, 1))
        for dt in range(DT):
            R.emit("pe", "matmul", U[0].ap, onesdiv_sb[:], cur[dt][:, 1:TT + 1],
                   start=(dt == 0), stop=(dt == DT - 1),
                   wait=[cast_v[dt]] + ([U[0].release, pre_io] if dt == 0 else []))
        g1 = R.emit("pe", "engine_nop", inc=(s_pe, 1))
        for dt in range(DT):
            R.emit("pe", "matmul", U[1].ap, onesdiv_sb[:], scr[dt][:],
                   start=(dt == 0), stop=(dt == DT - 1),
                   wait=[sq_v[dt]] + ([U[1].release] if dt == 0 else []))
        g2 = R.emit("pe", "engine_nop", inc=(s_pe, 1))
        R.emit("ve", "tensor_copy", mean_b[:], U[0].ap, wait=[g1])
        v_mean = R.emit("ve", "engine_nop", inc=(s_ve, 1))
        U[0].release = v_mean
        R.emit("ve", "tensor_mul", tmpA[:], mean_b[:], mean_b[:],
               wait=[ST["ln_act"]])
        R.emit("ve", "tensor_sub", tmpA[:], U[1].ap, tmpA[:], wait=[g2])
        v_var = R.emit("ve", "tensor_scalar_add", tmpA[:], tmpA[:], EPS_LN,
                       inc=(s_ve, 1))
        U[1].release = v_var
        if sub == 2:
            return v_var
        R.emit("act", "activation", rstd_b[:], tmpA[:], AF.Ln,
               wait=[v_var, ST["rstd"]])
        a_rstd = R.emit("act", "activation", rstd_b[:], rstd_b[:], AF.Exp,
                        scale=-0.5, inc=(s_act, 1))
        ST["ln_act"] = a_rstd
        ST["scr"] = g2
        first = {"ve": True, "gp": True}
        for dt in range(DT):
            eng = "gp" if dt in GPN else "ve"
            sem = s_gpc if eng == "gp" else s_ve
            tA = tmpB if eng == "ve" else a_sb
            w = [a_rstd, v_mean] if first[eng] else []
            first[eng] = False
            R.emit(eng, "tensor_sub", tA[:], x[dt][:], mean_b[:], wait=w)
            R.emit(eng, "tensor_mul", cur[dt][:, 1:TT + 1], tA[:], rstd_b[:],
                   inc=(sem, 1))
        v_gp = (s_gpc, s_gpc.count)
        if finalize:
            return R.emit("ve", "engine_nop", inc=(s_ve, 1))
        return None

    def emit_stage_a(seq, ch, xload):
        nxt = xnbs[ln_ctr[0] % 2]
        if ch == 0:
            R.emit("ve", "memset", carry_n[:], 0.0)
            R.emit("ve", "memset", carry_d[:], 0.0)
            for dt in range(DT):
                R.emit("ve", "memset", nxt[dt][:, 0:1], 0.0,
                       wait=[ST["xfin"][dt]])
        else:
            for dt in range(DT):
                R.emit("ve", "tensor_copy", nxt[dt][:, 0:1], xtail[:, dt:dt + 1],
                       wait=[ST["xfin"][dt]])
        v_gp = emit_ln(finalize=False, xload=xload)
        XN = ST["xnb"]
        for dt in range(DT):
            R.emit("ve", "tensor_copy", xtail[:, dt:dt + 1], XN[dt][:, TT:TT + 1],
                   wait=[v_gp] if dt == 0 else ())
        v_xn = R.emit("ve", "engine_nop", inc=(s_ve, 1))
        ST["rstd"] = v_xn

        rel = ST["a_rel"]
        for step in range(2 + DT):
            ce = step - 2
            if 0 <= ce < DT:
                gk, gv, gr, par = rel[ce]
                a_ek = rel[("ek", ce)]
                R.emit("ve", "tensor_mul", a_sb[:], V[par].ap, ek_sb[par][:],
                       wait=[gv, a_ek, vw])
                v_a = R.emit("ve", "engine_nop", inc=(s_ve, 1))
                V[par].release = v_a
                R.emit("ve", "tensor_tensor_scan", tmpA[:],
                       w_sb[:, ce:ce + 1].broadcast_to([128, TT]), a_sb[:],
                       carry_n[:, ce:ce + 1], ALU.mult, ALU.add)
                R.emit("ve", "tensor_tensor_scan", tmpB[:],
                       w_sb[:, ce:ce + 1].broadcast_to([128, TT]), ek_sb[par][:],
                       carry_d[:, ce:ce + 1], ALU.mult, ALU.add)
                v_scan = R.emit("ve", "engine_nop", inc=(s_ve, 1))
                rel[("ekfree", par)] = v_scan
                R.emit("ve", "tensor_copy", carry_n[:, ce:ce + 1], tmpA[:, TT - 1:TT])
                R.emit("ve", "tensor_copy", carry_d[:, ce:ce + 1], tmpB[:, TT - 1:TT])
                R.emit("ve", "tensor_scalar_add", tmpB[:], tmpB[:], 1e-8)
                R.emit("ve", "reciprocal", tmpB[:], tmpB[:])
                R.emit("ve", "tensor_mul", tmpA[:], tmpA[:], tmpB[:])
                R.emit("ve", "tensor_mul", a_sb[:], tmpA[:], W[par].ap, wait=[gr])
                v_fin = R.emit("ve", "tensor_add", x[ce][:], x[ce][:], a_sb[:],
                               inc=(s_ve, 1))
                W[par].release = v_fin
                ST["xfin"][ce] = v_fin
            if step < DT:
                et = step
                sk = load_col(ringK, "wkT", None, et, 2048)
                sv = load_col(ringK, "wvT", None, et, 2048)
                sr = load_col(ringK, "wrT", None, et, 2048)
                par = et % 2
                for kt in range(DT):
                    R.emit("pe", "matmul", U[par].ap, sk.t[:, kt * 128:(kt + 1) * 128],
                           XN[kt][:, 0:TT], start=(kt == 0), stop=(kt == DT - 1),
                           wait=([sk.ready, v_xn, U[par].release]
                                 if kt == 0 else ()))
                gk = R.emit("pe", "engine_nop", inc=(s_pe, 1))
                sk.release = gk
                for kt in range(DT):
                    R.emit("pe", "matmul", V[par].ap, sv.t[:, kt * 128:(kt + 1) * 128],
                           XN[kt][:, 0:TT], start=(kt == 0), stop=(kt == DT - 1),
                           wait=([sv.ready, V[par].release] if kt == 0 else ()))
                gv = R.emit("pe", "engine_nop", inc=(s_pe, 1))
                sv.release = gv
                for kt in range(DT):
                    R.emit("pe", "matmul", W[par].ap, sr.t[:, kt * 128:(kt + 1) * 128],
                           XN[kt][:, 1:TT + 1], start=(kt == 0), stop=(kt == DT - 1),
                           wait=([sr.ready, W[par].release] if kt == 0 else ()))
                gr = R.emit("pe", "engine_nop", inc=(s_pe, 1))
                sr.release = gr
                rel[et] = (gk, gv, gr, par)
                R.emit("act", "activation", ek_sb[par][:], U[par].ap, AF.Exp,
                       wait=[gk, rel.get(("ekfree", par))])
                a_ek = R.emit("act", "engine_nop", inc=(s_act, 1))
                U[par].release = a_ek
                rel[("ek", et)] = a_ek

    def emit_ffn():
        v_xn = emit_ln()
        ST["rstd"] = v_xn
        XN = ST["xnb"]
        for s in range(4):
            for kt in range(DT):
                R.emit("pe", "matmul", psW0[:, s * 8:(s + 1) * 8],
                       XN[kt][:, 1 + s * 128:1 + (s + 1) * 128],
                       gate_sb[:, kt * NEXP:(kt + 1) * NEXP],
                       start=(kt == 0), stop=(kt == DT - 1),
                       wait=([v_xn, W[0].release, gate_val] if (s == 0 and kt == 0) else ()))
        glog = R.emit("pe", "engine_nop", inc=(s_pe, 1))
        R.emit("ve", "tensor_copy", mrt[:], psW0[:, 0:32], wait=[glog])
        v_mrt = R.emit("ve", "engine_nop", inc=(s_ve, 1))
        W[0].release = v_mrt
        def vch(method, *args, wait=(), **kw):
            return R.emit("ve", method, *args,
                          wait=list(wait) + [(s_ve, s_ve.count)],
                          inc=(s_ve, 1), **kw)

        for s in range(4):
            sl = mrt[:, s * 8:(s + 1) * 8]
            c0 = r_small[:, s * 8:s * 8 + 1]
            c1 = r_small[:, s * 8 + 1:s * 8 + 2]
            c2 = r_small[:, s * 8 + 2:s * 8 + 3]
            c3 = r_small[:, s * 8 + 3:s * 8 + 4]
            vch("reduce_max", c0, sl, axis=AX.X, negate=True)
            vch("tensor_scalar_mul", c1, c0, -1.0)
            vch("tensor_scalar", eqb[:], sl, c1, None, ALU.is_ge)
            vch("scalar_tensor_tensor", mskb[:], eqb[:], NEG, sl,
                ALU.mult, ALU.add)
            v_msk = vch("reduce_max", c2, mskb[:], axis=AX.X)
            a_esh = R.emit("act", "activation", eshb[:], sl, AF.Exp, bias=c0,
                           wait=[v_msk], inc=(s_act, 1))
            vch("tensor_scalar", eqb[:], sl, c2, None, ALU.is_ge)
            vch("tensor_mul", mskb[:], eqb[:], eshb[:], wait=[a_esh])
            vch("reduce_sum", c3, mskb[:], axis=AX.X)
            vch("reciprocal", c3, c3)
            vch("tensor_scalar_mul", m_sb[:, s * 8:(s + 1) * 8], mskb[:], c3)
        v_m = R.emit("ve", "engine_nop", inc=(s_ve, 1))
        g_mT = R.emit("pe", "transpose", psW0[0:32, 128:256], m_sb[:], idf_sb[:],
                      wait=[v_m, W[0].release], inc=(s_pe, 1))
        v_mtb = R.emit("ve", "tensor_copy", mT_bf[:], psW0[0:32, 128:256],
                       wait=[g_mT], inc=(s_ve, 1))
        W[0].release = v_mtb

        u_rel = ST["u_rel"]

        def emit_y_group(st, ht):
            par = ht % 2
            ring = ring2 if st["shared"] else ringK
            s2 = load_col(ring, "sw2" if st["shared"] else "w2",
                          None if st["shared"] else st["e"], ht, st["nit"] * 128)
            for it in range(st["nit"]):
                R.emit("pe", "matmul", W[par].ap, s2.t[:, it * 128:(it + 1) * 128],
                       u_t[:, (st["uoff"] + it) * TT:(st["uoff"] + it + 1) * TT],
                       start=(it == 0), stop=(it == st["nit"] - 1),
                       wait=([s2.ready, W[par].release, st["uval"]] if it == 0 else ()))
            gy = R.emit("pe", "engine_nop", inc=(s_pe, 1))
            s2.release = gy
            v_y = R.emit("ve", "tensor_add", x[ht][:], x[ht][:], W[par].ap,
                         wait=[gy], inc=(s_ve, 1))
            W[par].release = v_y
            ST["xfin"][ht] = v_y
            st["last_gy"] = gy
            if not st["q"]:
                for it in range(st["nit"]):
                    u_rel[st["uoff"] + it] = gy

        def drain_y(st, n):
            while st is not None and st["q"] and n > 0:
                emit_y_group(st, st["q"].pop(0))
                n -= 1

        pend = None
        for e in range(NEXP + 1):
            shared = (e == NEXP)
            nit = ITS if shared else ITE
            uoff = 0 if shared else (e % 2) * ITE
            if not shared:
                for s in range(4):
                    j = s * 8 + e
                    R.emit("pe", "matmul", TB32[0][:, s * 128:(s + 1) * 128],
                           oh_sb[:, j * 128:(j + 1) * 128], mT_bf[:],
                           start=True, stop=True,
                           wait=([v_mtb, TB[0].release] if s == 0 else ()))
                g_be = R.emit("pe", "engine_nop", inc=(s_pe, 1))
                v_be = R.emit("ve", "tensor_copy", be_sb[:], TB32[0][:, 0:TT],
                              wait=[g_be], inc=(s_ve, 1))
                TB[0].release = v_be
            for it in range(nit):
                par = it % 2
                s1 = load_col(ringK, "sw1" if shared else "w1", None if shared else e, it, 2048)
                for kt in range(DT):
                    R.emit("pe", "matmul", U[par].ap, s1.t[:, kt * 128:(kt + 1) * 128],
                           XN[kt][:, 1:TT + 1], start=(kt == 0), stop=(kt == DT - 1),
                           wait=([s1.ready, v_xn, U[par].release] if kt == 0 else ()))
                gh = R.emit("pe", "engine_nop", inc=(s_pe, 1))
                s1.release = gh
                a_si = R.emit("act", "activation", silu_sb[par][:], U[par].ap, AF.Silu,
                              wait=[gh, u_rel.get(("silufree", par))], inc=(s_act, 1))
                U[par].release = a_si
                s3 = load_col(ringK, "sw3" if shared else "w3", None if shared else e, it, 2048)
                for kt in range(DT):
                    R.emit("pe", "matmul", V[par].ap, s3.t[:, kt * 128:(kt + 1) * 128],
                           XN[kt][:, 1:TT + 1], start=(kt == 0), stop=(kt == DT - 1),
                           wait=([s3.ready, V[par].release] if kt == 0 else ()))
                gg = R.emit("pe", "engine_nop", inc=(s_pe, 1))
                s3.release = gg
                drain_y(pend, 2)
                ut = u_t[:, (uoff + it) * TT:(uoff + it + 1) * TT]
                if shared:
                    v_u = R.emit("ve", "tensor_mul", ut, V[par].ap, silu_sb[par][:],
                                 wait=[gg, a_si, u_rel.get(uoff + it)], inc=(s_ve, 1))
                else:
                    R.emit("ve", "tensor_mul", tmpB[:], V[par].ap, silu_sb[par][:],
                           wait=[gg, a_si])
                    v_u = R.emit("ve", "tensor_mul", ut, tmpB[:], be_sb[:],
                                 wait=[u_rel.get(uoff + it)], inc=(s_ve, 1))
                V[par].release = v_u
                u_rel[("silufree", par)] = v_u
                u_rel[("uval", uoff + it)] = v_u
            drain_y(pend, DT)
            pend = {"e": e, "shared": shared, "nit": nit, "uoff": uoff,
                    "q": list(range(DT)), "last_gy": None,
                    "uval": u_rel[("uval", uoff + nit - 1)]}
        drain_y(pend, DT)
        return

    def emit_stage_c(seq, ch, rope_val):
        nonlocal rope_release
        v_xn = emit_ln()
        ST["rstd"] = v_xn
        XN = ST["xnb"]
        kv_done = None
        for g in range(NKV):
            par = g % 2
            sk = load_col(ringK, "wkb", None, g, 2048)
            for kt in range(DT):
                R.emit("pe", "matmul", U[par].ap, sk.t[:, kt * 128:(kt + 1) * 128],
                       XN[kt][:, 1:TT + 1], start=(kt == 0), stop=(kt == DT - 1),
                       wait=([sk.ready, v_xn, U[par].release] if kt == 0 else ()))
            gk = R.emit("pe", "engine_nop", inc=(s_pe, 1))
            sk.release = gk
            v_kp = R.emit("ve", "tensor_scalar", qkv_pre[:], U[par].ap,
                          bk_sb[:, g:g + 1], None, ALU.add,
                          wait=[gk, pre_io], inc=(s_ve, 1))
            g_sw = R.emit("pe", "matmul", U[par].ap, swap_sb[:], qkv_pre[:],
                          start=True, stop=True, wait=[v_kp], inc=(s_pe, 1))
            R.emit("ve", "tensor_mul", tmpA[:], qkv_pre[:], ropek_c[:],
                   wait=[rope_val])
            R.emit("ve", "tensor_mul", tmpB[:], U[par].ap, ropek_s[:], wait=[g_sw])
            v_kr = R.emit("ve", "tensor_add", kc[g][:, ch * TT:(ch + 1) * TT],
                          tmpA[:], tmpB[:], inc=(s_ve, 1))
            U[par].release = v_kr
            sv = load_col(ringK, "wvb", None, g, 2048)
            for kt in range(DT):
                R.emit("pe", "matmul", U[par].ap, sv.t[:, kt * 128:(kt + 1) * 128],
                       XN[kt][:, 1:TT + 1], start=(kt == 0), stop=(kt == DT - 1),
                       wait=([sv.ready, U[par].release] if kt == 0 else ()))
            gv = R.emit("pe", "engine_nop", inc=(s_pe, 1))
            sv.release = gv
            v_vp = R.emit("ve", "tensor_scalar", qkv_pre[:], U[par].ap,
                          bv_sb[:, g:g + 1], None, ALU.add, wait=[gv], inc=(s_ve, 1))
            U[par].release = v_vp
            for s in range(4):
                R.emit("pe", "transpose", TB[par].ap[:, s * 128:(s + 1) * 128],
                       qkv_pre[:, s * 128:(s + 1) * 128], idb_sb[:],
                       wait=([v_vp, TB[par].release] if s == 0 else ()))
            g_vt = R.emit("pe", "engine_nop", inc=(s_pe, 1))
            for s in range(4):
                R.emit("ve", "tensor_copy", vc[ch * 4 + s][:, g * 128:(g + 1) * 128],
                       TB[par].ap[:, s * 128:(s + 1) * 128], wait=[g_vt] if s == 0 else ())
            kv_done = R.emit("ve", "engine_nop", inc=(s_ve, 1))
            TB[par].release = kv_done
        SC = [W, V]
        SCP = [psW0, psV0]
        SCD = [psW1, psV1]

        def emit_scores(h, s, v_qr):
            i = h * 4 + s
            P2 = i % 2
            qs = ch * TT + s * 128
            Wp = min(qs, 512)
            g = h // 4
            scb = SC[P2]
            sctp, sctd = SCP[P2], SCD[P2]
            qpar = h % 2
            if Wp:
                R.emit("pe", "matmul", sctp[:, 0:Wp],
                       q_h[qpar][:, s * 128:(s + 1) * 128],
                       kc[g][:, qs - Wp:qs], start=True, stop=(Wp != 512),
                       wait=[v_qr, kv_done, scb[0].release, scb[1].release, pre_io])
                if Wp == 512:
                    R.emit("pe", "matmul", sctp[:, 0:512], idb_sb[:], mask_p[:],
                           start=False, stop=True)
            R.emit("pe", "matmul", sctd[:, 0:128],
                   q_h[qpar][:, s * 128:(s + 1) * 128],
                   kc[g][:, qs:qs + 128], start=True, stop=False,
                   wait=([v_qr, kv_done, scb[0].release, scb[1].release, pre_io]
                         if not Wp else ()))
            R.emit("pe", "matmul", sctd[:, 0:128], idb_sb[:], mask_d[:],
                   start=False, stop=True)
            return R.emit("pe", "engine_nop", inc=(s_pe, 1))

        def emit_chain(h, s, g_sc):
            i = h * 4 + s
            P2 = i % 2
            qs = ch * TT + s * 128
            Wp = min(qs, 512)
            nj = Wp // 128
            g = h // 4
            scb = SC[P2]
            sctp, sctd = SCP[P2], SCD[P2]
            ac = r_acc
            if Wp:
                R.emit("ve", "reduce_max", ac[:, 4:5], sctp[:, 0:Wp], axis=AX.X,
                       wait=[g_sc], inc=(s_ve, 1))
                R.emit("ve", "reduce_max", ac[:, 5:6], sctd[:, 0:128], axis=AX.X,
                       inc=(s_ve, 1))
                R.emit("ve", "tensor_max", ac[:, 4:5], ac[:, 4:5], ac[:, 5:6],
                       wait=[(s_ve, s_ve.count)], inc=(s_ve, 1))
                v_mx = R.emit("ve", "tensor_scalar_mul", ac[:, 6:7], ac[:, 4:5],
                              -1.0, wait=[(s_ve, s_ve.count)], inc=(s_ve, 1))
            else:
                v_mx = R.emit("ve", "reduce_max", ac[:, 6:7], sctd[:, 0:128],
                              axis=AX.X, negate=True, wait=[g_sc], inc=(s_ve, 1))
            if Wp:
                R.emit("act", "activation", p_sb[P2][:, 0:Wp], sctp[:, 0:Wp],
                       AF.Exp, bias=ac[:, 6:7], accum_out=ac[:, 0:1], wait=[v_mx])
            a_exp = R.emit("act", "activation", p_sb[P2][:, 512:640],
                           sctd[:, 0:128], AF.Exp, bias=ac[:, 6:7],
                           accum_out=ac[:, 1:2],
                           wait=[v_mx] if not Wp else (), inc=(s_act, 1))
            scb[0].release = scb[1].release = a_exp
            if Wp:
                R.emit("ve", "tensor_add", ac[:, 2:3], ac[:, 0:1], ac[:, 1:2],
                       wait=[a_exp], inc=(s_ve, 1))
            else:
                R.emit("ve", "tensor_copy", ac[:, 2:3], ac[:, 1:2],
                       wait=[a_exp], inc=(s_ve, 1))
            R.emit("ve", "reciprocal", ac[:, 2:3], ac[:, 2:3],
                   wait=[(s_ve, s_ve.count)], inc=(s_ve, 1))
            v_p = R.emit("ve", "tensor_scalar_mul", p_sb[P2][:, 0:640],
                         p_sb[P2][:, 0:640], ac[:, 2:3],
                         wait=[(s_ve, s_ve.count)], inc=(s_ve, 1))
            for j in range(nj):
                R.emit("pe", "transpose", TB[P2].ap[:, j * 128:(j + 1) * 128],
                       p_sb[P2][:, j * 128:(j + 1) * 128], idb_sb[:],
                       wait=([v_p, TB[P2].release] if j == 0 else ()))
            R.emit("pe", "transpose", TB[P2].ap[:, 512:640],
                   p_sb[P2][:, 512:640], idb_sb[:],
                   wait=([v_p, TB[P2].release] if nj == 0 else ()))
            g_tr = R.emit("pe", "engine_nop", inc=(s_pe, 1))
            v_pt = R.emit("ve", "tensor_copy", pT_sb[P2][:, 0:640],
                          TB[P2].ap[:, 0:640], wait=[g_tr], inc=(s_ve, 1))
            vt0 = (qs - Wp) // 128
            for j in range(nj + 1):
                vt = vt0 + j if j < nj else qs // 128
                rhs = pT_sb[P2][:, j * 128:(j + 1) * 128] if j < nj else \
                    pT_sb[P2][:, 512:640]
                R.emit("pe", "matmul", TB32[P2][:, 384:512],
                       vc[vt][:, g * 128:(g + 1) * 128], rhs,
                       start=(j == 0), stop=(j == nj),
                       wait=[v_pt] if j == 0 else ())
            g_av = R.emit("pe", "engine_nop", inc=(s_pe, 1))
            v_at = R.emit("ve", "tensor_copy",
                          scr[h][:, s * 128:(s + 1) * 128], TB32[P2][:, 384:512],
                          wait=[g_av], inc=(s_ve, 1))
            TB[P2].release = v_at

        def qproj_start(h):
            return {"h": h, "qpar": h % 2, "kt": 0,
                    "sq": load_col(ringK, "wq", None, h, 2048)}

        def qproj_part(stq, nk):
            sq, qpar = stq["sq"], stq["qpar"]
            hi = min(stq["kt"] + nk, DT)
            for kt in range(stq["kt"], hi):
                R.emit("pe", "matmul", U[qpar].ap, sq.t[:, kt * 128:(kt + 1) * 128],
                       XN[kt][:, 1:TT + 1], start=(kt == 0), stop=(kt == DT - 1),
                       wait=([sq.ready, v_xn, U[qpar].release] if kt == 0 else ()))
            stq["kt"] = hi
            if hi == DT and "gq" not in stq:
                stq["gq"] = R.emit("pe", "engine_nop", inc=(s_pe, 1))
                sq.release = stq["gq"]

        def qproj_finish(stq):
            qproj_part(stq, DT)
            h, qpar, gq = stq["h"], stq["qpar"], stq["gq"]
            v_qp = R.emit("ve", "tensor_scalar", qkv_pre[:], U[qpar].ap,
                          bq_sb[:, h:h + 1], None, ALU.add, wait=[gq], inc=(s_ve, 1))
            g_sw = R.emit("pe", "matmul", U[qpar].ap, swap_sb[:], qkv_pre[:],
                          start=True, stop=True, wait=[v_qp], inc=(s_pe, 1))
            R.emit("ve", "tensor_mul", tmpA[:], qkv_pre[:], ropeq_c[:], wait=[rope_val])
            R.emit("ve", "tensor_mul", tmpB[:], U[qpar].ap, ropeq_s[:], wait=[g_sw])
            v_qr = R.emit("ve", "tensor_add", q_h[qpar][:], tmpA[:], tmpB[:],
                          inc=(s_ve, 1))
            U[qpar].release = v_qr
            return v_qr

        stq = qproj_start(0)
        v_qr = qproj_finish(stq)
        for h in range(NQ):
            nxt = qproj_start(h + 1) if h + 1 < NQ else None
            g_sc = emit_scores(h, 0, v_qr)
            for s in range(4):
                if nxt is not None:
                    qproj_part(nxt, 4)
                g_next = emit_scores(h, s + 1, v_qr) if s < 3 else None
                emit_chain(h, s, g_sc)
                g_sc = g_next
            if nxt is not None:
                v_qr = qproj_finish(nxt)
        rope_release = v_qr
        v_attn = R.emit("ve", "engine_nop", inc=(s_ve, 1))
        for et in range(DT):
            par = et % 2
            so = load_col(ringK, "wo", None, et, 2048)
            for kt in range(DT):
                R.emit("pe", "matmul", U[par].ap, so.t[:, kt * 128:(kt + 1) * 128],
                       scr[kt][:], start=(kt == 0), stop=(kt == DT - 1),
                       wait=([so.ready, v_attn, U[par].release] if kt == 0 else ()))
            go = R.emit("pe", "engine_nop", inc=(s_pe, 1))
            so.release = go
            v_o = R.emit("ve", "scalar_tensor_tensor", x[et][:], U[par].ap,
                         bo_sb[:, et:et + 1], x[et][:], ALU.add, ALU.add,
                         wait=[go], inc=(s_ve, 1))
            U[par].release = v_o
            ST["xfin"][et] = v_o
            ST["scr"] = go

    def emit_xload(inst, per_tile_after=None):
        seq, ch = divmod(inst, NCH)
        for dt in range(DT):
            w = []
            if per_tile_after is not None:
                w.append(per_tile_after[dt])
            elif dt == 0 and io_release is not None:
                w.append(io_release)
            R.emit("sp", "dma_start", x[dt][:],
                   d_xT[seq, dt * 128:(dt + 1) * 128, ch * TT:(ch + 1) * TT],
                   wait=w, inc=(s_x, 16))
        return (s_x, s_x.count)

    xload = emit_xload(0)
    for inst in range(N_INST):
        seq, ch = divmod(inst, NCH)
        for dst, src in [(ropeq_c, dC["ropeqc"]), (ropeq_s, dC["ropeqs"]),
                         (ropek_c, dC["ropekc"]), (ropek_s, dC["ropeks"])]:
            R.emit("sp", "dma_start", dst[:], src[:, ch * TT:(ch + 1) * TT],
                   wait=[rope_release] if dst is ropeq_c else (), inc=(s_rope, 16))
        rope_val = (s_rope, s_rope.count)

        if STOP_STAGE == "P":
            pass
        elif STOP_STAGE in ("L", "L1", "L2", "L3"):
            barrier(extra=[xload])
            emit_ln(finalize=True, xload=xload,
                    sub={"L1": 1, "L2": 2, "L3": 3}.get(STOP_STAGE))
        else:
            emit_stage_a(seq, ch, xload)
        if STOP_STAGE not in ("A", "P", "L", "L1", "L2", "L3"):
            emit_ffn()
            if STOP_STAGE != "B":
                emit_stage_c(seq, ch, rope_val)
                if STOP_STAGE != "C":
                    emit_ffn()
        if STOP_STAGE is None:
            after = []
            for dt in range(DT):
                R.emit("sp", "dma_start",
                       d_out[seq, dt * 128:(dt + 1) * 128, ch * TT:(ch + 1) * TT],
                       x[dt][:], wait=[ST["xfin"][dt]], inc=(s_y, 16))
                after.append((s_y, s_y.count))
            io_release = (s_y, s_y.count)
            if inst + 1 < N_INST:
                xload = emit_xload(inst + 1, per_tile_after=after)
        else:
            v_last = R.emit("ve", "engine_nop", inc=(s_ve, 1))
            for dt in range(DT):
                R.emit("sp", "dma_start",
                       d_out[seq, dt * 128:(dt + 1) * 128, ch * TT:(ch + 1) * TT],
                       x[dt][:], wait=[v_last] if dt == 0 else (), inc=(s_y, 16))
            io_release = (s_y, s_y.count)
            if inst + 1 < N_INST:
                xload = emit_xload(inst + 1)

    R.emit("sp", None, wait=[(s_y, s_y.count), (s_pre, s_pre.count),
                             (s_x, s_x.count), (s_rope, s_rope.count)])
    gp_fin = [(s_gate, s_gate.count)]
    gp_fin += [(s.sem, 16 * s.nloads) for s in ringK.slots + ring2.slots]
    R.emit("gp", None, wait=gp_fin)

    blk_cm = nc.Block()
    blk = blk_cm.__enter__()

    def _mk(engkey):
        def f(eng):
            for method, args, kwargs, waits, rinc in R.ops[engkey]:
                for sem, val in waits:
                    eng.wait_ge(sem.h, val)
                if method is None:
                    continue
                if method == "engine_nop":
                    ins = eng.nop()
                else:
                    ins = getattr(eng, method)(*args, **kwargs)
                if rinc is not None:
                    ins.then_inc(rinc[0].h, rinc[1])
        return f

    blk.tensor(_mk("pe"))
    blk.vector(_mk("ve"))
    blk.scalar(_mk("act"))
    blk.sync(_mk("sp"))
    blk.gpsimd(_mk("gp"))
    blk_cm.__exit__(None, None, None)
    return nc


def kernel(**inputs):
    xT = np.ascontiguousarray(
        np.asarray(inputs["L0_attn_a_x"], dtype=np.float32).transpose(0, 2, 1))
    packs = _host_pack(inputs)
    nc = _build_nc()
    in_maps = []
    for c in range(8):
        m = dict(packs)
        m["xT"] = np.ascontiguousarray(xT[2 * c:2 * c + 2])
        in_maps.append(m)
    res = run_bass_kernel_spmd(nc, in_maps, core_ids=list(range(8)))
    out = np.empty((16, 2048, 2048), dtype=np.float32)
    for c in range(8):
        out[2 * c:2 * c + 2] = res.results[c]["y"].transpose(0, 2, 1)
    return out
